# version 11
# baseline (speedup 1.0000x reference)
import numpy as np
import ml_dtypes
import concourse.bass as bass
import concourse.mybir as mybir
from concourse.bass_utils import run_bass_kernel_spmd
import contextlib

F32 = mybir.dt.float32
BF16 = mybir.dt.bfloat16
I32 = mybir.dt.int32
AF = mybir.ActivationFunctionType
ALU = mybir.AluOpType
AX = mybir.AxisListType

MADD = 0x1FFFFF
MMASK = -4194304
SGNMASK = -2147483648
NEG = -1.0e10

FULL = dict(T=4096, D=1024, H=16, II=2816)
SMALL = dict(T=256, D=256, H=4, II=256)
MED = dict(T=1024, D=512, H=8, II=1024)

_DEBUG = False


def _chunks(total, step):
    out, off = [], 0
    while off < total:
        s = min(step, total - off)
        out.append((off, s))
        off += s
    return out


def build_nc(C, debug=False, maxph=99, f1mode=0):
    T, D, H, II = C["T"], C["D"], C["H"], C["II"]
    HD = D // H
    assert HD == 64
    NT = T // 128
    ND = D // 128
    NIC = II // 128
    PAD = 128
    GI = II // 16
    ICH = _chunks(II, 512)
    DCH = _chunks(D, 512)
    NIH = len(ICH)
    NTH = NT // 2

    nc = bass.Bass("TRN2", target_bir_lowering=False,
                   dynamic_dma_scratch_size=16384 + 4096)

    x = nc.dram_tensor("x", [T, D], F32, kind="ExternalInput").ap()
    wg = nc.dram_tensor("wg", [D, D], F32, kind="ExternalInput").ap()
    wv = nc.dram_tensor("wv", [D, D], F32, kind="ExternalInput").ap()
    wo = nc.dram_tensor("wo", [D, D], F32, kind="ExternalInput").ap()
    bgr = nc.dram_tensor("bgr", [1, D], F32, kind="ExternalInput").ap()
    bvr = nc.dram_tensor("bvr", [1, D], F32, kind="ExternalInput").ap()
    bor = nc.dram_tensor("bor", [1, D], F32, kind="ExternalInput").ap()
    wgate = nc.dram_tensor("wgate", [II, D], F32, kind="ExternalInput").ap()
    wup = nc.dram_tensor("wup", [II, D], F32, kind="ExternalInput").ap()
    wdown = nc.dram_tensor("wdown", [D, II], F32, kind="ExternalInput").ap()
    tri = nc.dram_tensor("tri", [2, 128, 128], BF16, kind="ExternalInput").ap()
    negm = nc.dram_tensor("negm", [2, 128, 256], F32, kind="ExternalInput").ap()
    onesb = nc.dram_tensor("onesb", [1, 128], BF16, kind="ExternalInput").ap()
    onesc = nc.dram_tensor("onesc", [128, 1], F32, kind="ExternalInput").ap()
    onesr = nc.dram_tensor("onesr", [1, 128], F32, kind="ExternalInput").ap()
    icon = nc.dram_tensor("icon", [128, 6], I32, kind="ExternalInput").ap()
    fcon = nc.dram_tensor("fcon", [128, 2], F32, kind="ExternalInput").ap()

    skind = "ExternalOutput" if debug else "Internal"
    vsp = nc.dram_tensor("vsp", [NT, 128, D], BF16, kind=skind).ap()
    xqt_d = nc.dram_tensor("xqt_d", [ND, 128, T], BF16, kind=skind).ap()
    xb_d = nc.dram_tensor("xb_d", [T, D], F32, kind=skind).ap()
    xn_d = nc.dram_tensor("xn_d", [T, D], F32, kind=skind).ap() if debug else None
    lns_d = nc.dram_tensor("lns_d", [NT, 128, 8], F32, kind=skind).ap() if debug else None
    at_d = nc.dram_tensor("at_d", [NT, 128, D], BF16, kind=skind).ap()
    part_d = nc.dram_tensor("part_d", [T, D], F32, kind=skind).ap()
    wgq_d = nc.dram_tensor("wgq_d", [ND, 128, II], BF16, kind=skind).ap()
    wuq_d = nc.dram_tensor("wuq_d", [ND, 128, II], BF16, kind=skind).ap()
    wdq_d = nc.dram_tensor("wdq_d", [NIC, 128, D], BF16, kind=skind).ap()
    hqt_d = nc.dram_tensor("hqt_d", [NT, 128, II], BF16, kind=skind).ap()
    o = nc.dram_tensor("o", [T, D], F32, kind="ExternalOutput").ap()

    es = contextlib.ExitStack()
    with es:
        g_ = es.enter_context

        sems = {}

        def S(name):
            h = g_(nc.semaphore(name))
            sems[name] = h
            return h

        s_cl = S("s_cl"); s_wc = S("s_wc")
        s_bb = S("s_bb"); s_bbe = S("s_bbe")
        s_var = S("s_var"); s_rstd = S("s_rstd")
        s_mb = S("s_mb"); s_xnf = S("s_xnf"); s_xnb = S("s_xnb")
        s_xq = S("s_xq")
        s_xb = S("s_xb")
        s_gmm = S("s_gmm"); s_vmm = S("s_vmm"); s_gev = S("s_gev")
        s_vev = S("s_vev"); s_gsq = S("s_gsq"); s_nsq = S("s_nsq")
        s_rn = S("s_rn"); s_ub = S("s_ub")
        s_wmm = S("s_wmm"); s_dcr = S("s_dcr")
        s_st = S("s_st")
        s_sc = S("s_sc"); s_nm = S("s_nm"); s_exp = S("s_exp")
        s_rd = S("s_rd")
        s_av = S("s_av"); s_ave = S("s_ave")
        s_woc = S("s_woc")
        s_p3m = S("s_p3m"); s_p3e = S("s_p3e")
        s_fwq = S("s_fwq")
        s_f1w = S("s_f1w"); s_gmf = S("s_gmf")
        s_silu = S("s_silu"); s_umf = S("s_umf"); s_hm = S("s_hm")
        s_hq = S("s_hq")
        s_f2w = S("s_f2w")
        s_dmm = S("s_dmm"); s_oe = S("s_oe")
        BARS = {e: S(f"bar_{e}") for e in ("t", "v", "s", "sy", "g")}

        blk = g_(nc.Block())
        CONSTS = g_(nc.sbuf_tensor("CONSTS", [128, 2], F32, side="right"))
        ICONs = g_(nc.sbuf_tensor("ICONs", [128, 6], I32, side="right"))

        NBAR = [0]
        PH_DMA = []

        def bwait(eng):
            if NBAR[0] > 0:
                for b in BARS.values():
                    eng.wait_ge(b, NBAR[0])
            for ds, total in PH_DMA:
                if isinstance(ds, list):
                    dwait_all(eng, ds, total)
                else:
                    eng.wait_ge(ds, 16 * total)

        def binc(eng, which):
            eng.drain().then_inc(BARS[which], 1)

        class DW:
            _OPS = {"tensor_tensor", "tensor_scalar", "scalar_tensor_tensor",
                    "tensor_copy", "tensor_reduce", "reciprocal", "memset",
                    "activation"}

            def __init__(self, eng):
                self._e = eng

            def __getattr__(self, n):
                a = getattr(self._e, n)
                if n in DW._OPS:
                    def w(*args, **kw):
                        a(*args, **kw)
                        return self._e.drain()
                    return w
                return a

        def DS(name, n=2):
            return [S(f"{name}_{i}") for i in range(n)]

        def dinc(inst, ds, k):
            inst.then_inc(ds[k % len(ds)], 16)

        def dwait(eng, ds, k):
            eng.wait_ge(ds[k % len(ds)], 16 * (k // len(ds) + 1))

        def emit_recip(v, out_ap, in_ap, s_ap, n):
            v.tensor_tensor(out=out_ap.bitcast(I32),
                            in0=ICONs[:, 4:5].broadcast_to([128, n]),
                            in1=in_ap.bitcast(I32), op=ALU.subtract)
            for _ in range(3):
                v.tensor_tensor(out=s_ap, in0=in_ap, in1=out_ap, op=ALU.mult)
                v.tensor_scalar(out=s_ap, in0=s_ap, scalar1=-1.0, scalar2=2.0,
                                op0=ALU.mult, op1=ALU.add)
                v.tensor_tensor(out=out_ap, in0=out_ap, in1=s_ap, op=ALU.mult)

        def dwait_all(eng, ds, total):
            n = len(ds)
            for i in range(n):
                cnt = (total - i + n - 1) // n
                if cnt > 0:
                    eng.wait_ge(ds[i], 16 * cnt)

        s_wl = DS("s_wl"); s_wt = DS("s_wt")
        s_x = DS("s_x"); s_xnt = DS("s_xnt"); s_xqt = DS("s_xqt")
        s_xqs = DS("s_xqs"); s_xbs = DS("s_xbs"); s_vsp = DS("s_vsp")
        s_ut = DS("s_ut")
        s_emt = DS("s_emt"); s_vw = DS("s_vw"); s_ats = DS("s_ats")
        s_wol = DS("s_wol"); s_wot = DS("s_wot")
        s_att = DS("s_att"); s_xbl = DS("s_xbl"); s_ps = DS("s_ps")
        s_fwl = DS("s_fwl"); s_fwt = DS("s_fwt"); s_fws = DS("s_fws")
        s_f1x = DS("s_f1x"); s_f2h = DS("s_f2h"); s_f2p = DS("s_f2p")
        s_xnd = DS("s_xnd")
        s_hqt = DS("s_hqt"); s_hqs = DS("s_hqs"); s_os = DS("s_os")

        scU = contextlib.ExitStack()
        u_ = scU.enter_context
        uT = u_(nc.sbuf_tensor("uT", [128, ND, PAD + T], BF16))
        TRIs = u_(nc.sbuf_tensor("TRIs", [128, 2, 128], BF16))
        NEGMs = u_(nc.sbuf_tensor("NEGMs", [128, 2, 256], F32))
        ONESC = u_(nc.sbuf_tensor("ONESC", [128, 1], F32))
        ONESR = u_(nc.sbuf_tensor("ONESR", [1, 128], F32))
        Rbuf = u_(nc.sbuf_tensor("Rbuf", [128, NT, H], F32))
        den = u_(nc.sbuf_tensor("den", [128, NT, H], F32))
        gf = u_(nc.sbuf_tensor("gf", [128, NT, H], F32))
        gfh = u_(nc.sbuf_tensor("gfh", [128, NT, H], F32))
        sinv = u_(nc.sbuf_tensor("sinv", [1, H], F32))
        sinvb = u_(nc.sbuf_tensor("sinvb", [128, H], F32))

        scW = contextlib.ExitStack()
        w_ = scW.enter_context
        WgT = w_(nc.sbuf_tensor("WgT", [128, ND, D], BF16))
        WvT = w_(nc.sbuf_tensor("WvT", [128, ND, D], BF16))
        ONESB = w_(nc.sbuf_tensor("ONESB", [1, 128], BF16))
        BROW = w_(nc.sbuf_tensor("BROW", [1, 3, D], F32))
        BROWB = w_(nc.sbuf_tensor("BROWB", [1, 2, D], BF16))
        BOB = w_(nc.sbuf_tensor("BOB", [128, D], F32))

        sc0 = contextlib.ExitStack()
        p0_ = sc0.enter_context
        wtmp = p0_(nc.sbuf_tensor("wtmp", [128, 2, D], F32))
        wbf = p0_(nc.sbuf_tensor("wbf", [128, 2, D], BF16))
        pB = p0_(nc.psum_tensor("pB", [128, 512], F32))

        @blk.sync
        def _(sy):
            sy.dma_start(out=TRIs[:], in_=tri.rearrange("t p q -> p t q")).then_inc(s_cl, 16)
            sy.dma_start(out=NEGMs[:], in_=negm.rearrange("t p q -> p t q")).then_inc(s_cl, 16)
            sy.dma_start(out=ONESB[:], in_=onesb).then_inc(s_cl, 16)
            sy.dma_start(out=ONESC[:], in_=onesc).then_inc(s_cl, 16)
            sy.dma_start(out=ONESR[:], in_=onesr).then_inc(s_cl, 16)
            sy.dma_start(out=ICONs[:], in_=icon).then_inc(s_cl, 16)
            sy.dma_start(out=CONSTS[:], in_=fcon).then_inc(s_cl, 16)
            for i, src in enumerate((bgr, bvr, bor)):
                sy.dma_start(out=BROW[:, i, :], in_=src).then_inc(s_cl, 16)
            for wi in range(2 * ND):
                sy.wait_ge(s_wc, wi + 1)
                dst = WgT if wi < ND else WvT
                r = wi % ND
                dinc(sy.dma_start_transpose(dst[:, :, r * 128:(r + 1) * 128],
                                            wbf[:, wi % 2, :]), s_wt, wi)
            dwait_all(sy, s_wt, 2 * ND)
            binc(sy, 'sy')

        @blk.scalar
        def _(s):
            s = DW(s)
            from concourse.hw_specs import get_activation_tables
            _idx = [i for i, (nm, fs) in enumerate(
                get_activation_tables(nc.m.arch).items())
                if AF.Ln in fs and AF.Exp in fs][0]
            nc.scalar.add_instruction(
                mybir.InstLoadActFuncSet(
                    name=nc.get_next_instruction_name(),
                    act_func_set_id=_idx, ins=[], outs=[]))
            for wi in range(2 * ND):
                src = wg if wi < ND else wv
                r = wi % ND
                if wi >= 2:
                    s.wait_ge(s_wc, wi - 1)
                dinc(s.dma_start(out=wtmp[:, wi % 2, :],
                                 in_=src[r * 128:(r + 1) * 128, :]), s_wl, wi)
            binc(s, 's')

        @blk.vector
        def _(v):
            v = DW(v)
            v.memset(uT[:, :, 0:PAD], 0.0)
            v.wait_ge(s_cl, 16 * 10)
            v.tensor_copy(out=BROWB[:, 0, :], in_=BROW[:, 0, :])
            v.tensor_copy(out=BROWB[:, 1, :], in_=BROW[:, 1, :])
            for wi in range(2 * ND):
                dwait(v, s_wl, wi)
                if wi >= 2:
                    dwait(v, s_wt, wi - 2)
                v.tensor_copy(out=wbf[:, wi % 2, :],
                              in_=wtmp[:, wi % 2, :]).then_inc(s_wc, 1)
            for ci, (co, cw) in enumerate(DCH):
                v.wait_ge(s_bb, ci + 1)
                v.tensor_copy(out=BOB[:, co:co + cw],
                              in_=pB[:, 0:cw]).then_inc(s_bbe, 1)
            binc(v, 'v')

        @blk.tensor
        def _(t):
            t.wait_ge(s_cl, 16 * 10)
            t.nop()
            for ci, (co, cw) in enumerate(DCH):
                if ci > 0:
                    t.wait_ge(s_bbe, ci)
                    t.nop()
                t.matmul(out=pB[:, 0:cw], lhsT=ONESR[:],
                         rhs=BROW[:, 2, co:co + cw], start=True, stop=True)
                t.drain().then_inc(s_bb, 1)
            binc(t, 't')

        @blk.gpsimd
        def _(g):
            g = DW(g)
            binc(g, 'g')

        NBAR[0] += 1
        PH_DMA.extend([(s_cl, 10), (s_wl, 2 * ND), (s_wt, 2 * ND)])
        sc0.close()
        if maxph == 0:
            scW.close()
            scU.close()
            return nc

        sc1 = contextlib.ExitStack()
        p1_ = sc1.enter_context
        xt = p1_(nc.sbuf_tensor("xt", [128, 2, D], F32))
        xnf = p1_(nc.sbuf_tensor("xnf", [128, 2, D], F32))
        xnb = p1_(nc.sbuf_tensor("xnb", [128, 2, D], BF16))
        xnTt = p1_(nc.sbuf_tensor("xnTt", [128, 2, ND, 128], BF16))
        xqb = p1_(nc.sbuf_tensor("xqb", [128, 2, D], BF16))
        txq = p1_(nc.sbuf_tensor("txq", [128, 2, ND, 128], BF16))
        q1 = p1_(nc.sbuf_tensor("q1", [128, D], F32))
        q2 = p1_(nc.sbuf_tensor("q2", [128, D], F32))
        q3 = p1_(nc.sbuf_tensor("q3", [128, D], F32))
        qgm = p1_(nc.sbuf_tensor("qgm", [128, 4, D // 16], F32))
        lns = p1_(nc.sbuf_tensor("lns", [128, 8], F32))
        gval = p1_(nc.sbuf_tensor("gval", [128, D], F32))
        gsqt = p1_(nc.sbuf_tensor("gsqt", [128, D], F32))
        nsqt = p1_(nc.sbuf_tensor("nsqt", [128, H], F32))
        rnt = p1_(nc.sbuf_tensor("rnt", [128, H], F32))
        ub = p1_(nc.sbuf_tensor("ub", [128, 2, D], BF16))
        vt = p1_(nc.sbuf_tensor("vt", [128, 2, D], BF16))
        xbt = p1_(nc.sbuf_tensor("xbt", [128, 2, D], F32))
        pG = p1_(nc.psum_tensor("pG", [128, D], F32))
        pV = p1_(nc.psum_tensor("pV", [128, D], F32))
        pW = p1_(nc.psum_tensor("pW", [128, D], F32))

        def emit_qdq(v, src, sgnsrc, out_ap, t1, t2, t3, gmt, Wd, nops=99):
            last = None
            G = Wd // 16
            src3 = src.rearrange("p (g k) -> p g k", k=16)
            if nops >= 1:
                last = v.tensor_reduce(out=gmt[:, 0, 0:G].unsqueeze(2), in_=src3,
                                axis=AX.X, op=ALU.max, apply_absolute_value=True)
            if nops >= 2:
                last = v.tensor_scalar(out=gmt[:, 1, 0:G], in0=gmt[:, 0, 0:G],
                                scalar1=6e-12, scalar2=None, op0=ALU.max)
            if nops >= 3:
                emit_recip(v, gmt[:, 3, 0:G], gmt[:, 1, 0:G], gmt[:, 0, 0:G], G)
                last = None
            if nops >= 4:
                last = v.tensor_scalar(out=gmt[:, 2, 0:G], in0=gmt[:, 1, 0:G],
                                scalar1=1.0 / 12.0, scalar2=None, op0=ALU.mult)
            if nops >= 5:
                last = v.tensor_scalar(out=gmt[:, 1, 0:G], in0=gmt[:, 3, 0:G],
                                scalar1=6.0, scalar2=None, op0=ALU.mult)
            iscB = gmt[:, 1, 0:G].unsqueeze(2).broadcast_to([128, G, 16])
            shB = gmt[:, 2, 0:G].bitcast(I32).unsqueeze(2).broadcast_to([128, G, 16])
            if nops >= 6:
                last = v.tensor_scalar(out=t1[:, 0:Wd].bitcast(I32),
                                in0=src.bitcast(I32), scalar1=ICONs[:, 3:4],
                                scalar2=None, op0=ALU.bitwise_and)
            if nops >= 7:
                last = v.tensor_tensor(out=t1[:, 0:Wd].rearrange("p (g k) -> p g k", k=16),
                                in0=t1[:, 0:Wd].rearrange("p (g k) -> p g k", k=16),
                                in1=iscB, op=ALU.mult)
            if nops >= 8:
                last = v.tensor_scalar(out=t2[:, 0:Wd], in0=t1[:, 0:Wd], scalar1=1.0,
                                scalar2=None, op0=ALU.max)
            if nops >= 9:
                last = v.tensor_tensor(out=t2[:, 0:Wd].bitcast(I32),
                                in0=t2[:, 0:Wd].bitcast(I32),
                                in1=ICONs[:, 0:1].broadcast_to([128, Wd]),
                                op=ALU.add)
            if nops >= 10:
                last = v.tensor_scalar(out=t2[:, 0:Wd].bitcast(I32),
                                in0=t2[:, 0:Wd].bitcast(I32),
                                scalar1=ICONs[:, 1:2], scalar2=None,
                                op0=ALU.bitwise_and)
            if nops >= 11:
                last = v.tensor_scalar(out=t3[:, 0:Wd], in0=t1[:, 0:Wd], scalar1=0.25,
                                scalar2=None, op0=ALU.is_gt)
            if nops >= 12:
                last = v.scalar_tensor_tensor(out=t3[:, 0:Wd], in0=t1[:, 0:Wd],
                                       scalar=0.75, in1=t3[:, 0:Wd],
                                       op0=ALU.is_gt, op1=ALU.add)
            if nops >= 13:
                last = v.tensor_scalar(out=t1[:, 0:Wd].bitcast(I32),
                                in0=sgnsrc.bitcast(I32), scalar1=ICONs[:, 2:3],
                                scalar2=None, op0=ALU.bitwise_and)
            if nops >= 14:
                last = v.tensor_tensor(out=t1[:, 0:Wd].bitcast(I32).rearrange("p (g k) -> p g k", k=16),
                                in0=t1[:, 0:Wd].bitcast(I32).rearrange("p (g k) -> p g k", k=16),
                                in1=shB, op=ALU.bitwise_or)
            if nops >= 15:
                last = v.tensor_tensor(out=t3[:, 0:Wd], in0=t3[:, 0:Wd],
                                in1=t1[:, 0:Wd], op=ALU.mult)
            if nops >= 16:
                last = v.tensor_tensor(out=out_ap, in0=t2[:, 0:Wd],
                                       in1=t3[:, 0:Wd], op=ALU.mult)
            return last

        @blk.sync
        def _(sy):
            bwait(sy)
            for m in range(NT):
                if m >= 2:
                    sy.wait_ge(s_xb, m - 1)
                dinc(sy.dma_start(out=xt[:, m % 2, :],
                                  in_=x[m * 128:(m + 1) * 128, :]), s_x, m)
                sy.wait_ge(s_xnb, m + 1)
                if m >= 2:
                    sy.wait_ge(s_vmm, m - 1)
                dinc(sy.dma_start_transpose(xnTt[:, m % 2, :, :],
                                            xnb[:, m % 2, :]), s_xnt, m)
                sy.wait_ge(s_xq, m + 1)
                if m >= 2:
                    dwait(sy, s_xqs, m - 2)
                dinc(sy.dma_start_transpose(txq[:, m % 2, :, :],
                                            xqb[:, m % 2, :]), s_xqt, m)
                sy.wait_ge(s_ub, m + 1)
                dinc(sy.dma_start_transpose(uT[:, :, PAD + m * 128:PAD + (m + 1) * 128],
                                            ub[:, m % 2, :]), s_ut, m)
            dwait_all(sy, s_ut, NT)
            binc(sy, 'sy')

        @blk.vector
        def _(v):
            v = DW(v)
            bwait(v)
            for m in range(NT):
                dwait(v, s_x, m)
                if m >= 1:
                    v.wait_ge(s_rstd, m)
                v.tensor_reduce(out=lns[:, 0:1], in_=xt[:, m % 2, :],
                                axis=AX.X, op=ALU.add)
                v.wait_ge(s_gsq, 2 * m)
                v.tensor_tensor(out=gsqt[:], in0=xt[:, m % 2, :],
                                in1=xt[:, m % 2, :], op=ALU.mult)
                v.tensor_reduce(out=lns[:, 1:2], in_=gsqt[:], axis=AX.X,
                                op=ALU.add)
                v.tensor_scalar(out=lns[:, 2:3], in0=lns[:, 0:1],
                                scalar1=1.0 / D, scalar2=None, op0=ALU.mult)
                v.tensor_scalar(out=lns[:, 3:4], in0=lns[:, 1:2],
                                scalar1=1.0 / D, scalar2=None, op0=ALU.mult)
                v.scalar_tensor_tensor(out=lns[:, 4:5], in0=lns[:, 2:3],
                                       scalar=-1.0, in1=lns[:, 2:3],
                                       op0=ALU.mult, op1=ALU.mult)
                v.tensor_tensor(out=lns[:, 3:4], in0=lns[:, 3:4],
                                in1=lns[:, 4:5], op=ALU.add).then_inc(s_var, 1)
                v.wait_ge(s_rstd, m + 1)
                v.scalar_tensor_tensor(out=lns[:, 6:7], in0=lns[:, 2:3],
                                       scalar=-1.0, in1=lns[:, 5:6],
                                       op0=ALU.mult, op1=ALU.mult).then_inc(s_mb, 1)
                v.wait_ge(s_xnf, m + 1)
                if m >= 2:
                    dwait(v, s_xnt, m - 2)
                v.tensor_copy(out=xnb[:, m % 2, :],
                              in_=xnf[:, m % 2, :]).then_inc(s_xnb, 1)
                if m >= 2:
                    dwait(v, s_xqt, m - 2)
                emit_qdq(v, xnf[:, m % 2, :], xnf[:, m % 2, :],
                         xqb[:, m % 2, :], q1, q2, q3, qgm, D).then_inc(s_xq, 1)
                if m >= 2:
                    dwait(v, s_xbs, m - 2)
                v.tensor_tensor(out=xbt[:, m % 2, :], in0=xt[:, m % 2, :],
                                in1=BOB[:], op=ALU.add).then_inc(s_xb, 1)
                v.wait_ge(s_gsq, 2 * m + 2)
                v.tensor_reduce(out=nsqt[:, 0:H].unsqueeze(2),
                                in_=gsqt[:].rearrange("p (h d) -> p h d", d=HD),
                                axis=AX.X, op=ALU.add).then_inc(s_nsq, 1)
                v.wait_ge(s_rn, m + 1)
                if m >= 1:
                    v.wait_ge(s_wmm, m)
                if m >= 2:
                    dwait(v, s_ut, m - 2)
                rnB = rnt[:, 0:H].unsqueeze(2).broadcast_to([128, H, HD])
                v.tensor_tensor(out=ub[:, m % 2, :].rearrange("p (h d) -> p h d", d=HD),
                                in0=gval[:].rearrange("p (h d) -> p h d", d=HD),
                                in1=rnB, op=ALU.mult).then_inc(s_ub, 1)
                v.wait_ge(s_wmm, m + 1)
                for co, cw in DCH:
                    v.tensor_tensor(out=gsqt[:, co:co + cw],
                                    in0=pW[:, co:co + cw],
                                    in1=ub[:, m % 2, co:co + cw], op=ALU.mult)
                v.tensor_reduce(out=Rbuf[:, m, 0:H].unsqueeze(2),
                                in_=gsqt[:].rearrange("p (h d) -> p h d", d=HD),
                                axis=AX.X, op=ALU.add).then_inc(s_dcr, 1)
            binc(v, 'v')

        @blk.scalar
        def _(s):
            s = DW(s)
            bwait(s)
            for m in range(NT):
                s.wait_ge(s_var, m + 1)
                s.activation(out=lns[:, 7:8], in_=lns[:, 3:4], func=AF.Ln,
                             bias=CONSTS[:, 1:2], scale=1.0)
                s.activation(out=lns[:, 5:6], in_=lns[:, 7:8], func=AF.Exp,
                             bias=CONSTS[:, 0:1], scale=-0.5).then_inc(s_rstd, 1)
                s.wait_ge(s_mb, m + 1)
                if m >= 2:
                    s.wait_ge(s_xq, m - 1)
                    s.wait_ge(s_xnb, m - 1)
                    if debug:
                        s.wait_ge(s_xnd[(m - 2) % 2], 32 * ((m - 2) // 2 + 1))
                s.activation(out=xnf[:, m % 2, :], in_=xt[:, m % 2, :],
                             func=AF.Identity, bias=lns[:, 6:7],
                             scale=lns[:, 5:6]).then_inc(s_xnf, 1)
                s.wait_ge(s_gmm, m + 1)
                s.wait_ge(s_ub, m)
                for co, cw in DCH:
                    s.activation(out=gval[:, co:co + cw], in_=pG[:, co:co + cw],
                                 func=AF.Identity, bias=CONSTS[:, 0:1], scale=1.0)
                s.nop().then_inc(s_gev, 1)
                s.wait_ge(s_dcr, m)
                s.activation(out=gsqt[:, 0:D], in_=gval[:, 0:D],
                             func=AF.Square, bias=CONSTS[:, 0:1],
                             scale=1.0).then_inc(s_gsq, 2)
                s.wait_ge(s_nsq, m + 1)
                s.activation(out=rnt[:, 0:H], in_=nsqt[:, 0:H], func=AF.Ln,
                             bias=CONSTS[:, 0:1], scale=1.0)
                s.activation(out=rnt[:, 0:H], in_=rnt[:, 0:H], func=AF.Exp,
                             bias=CONSTS[:, 0:1], scale=-0.5).then_inc(s_rn, 1)
                s.wait_ge(s_vmm, m + 1)
                if m >= 2:
                    dwait(s, s_vsp, m - 2)
                for co, cw in DCH:
                    s.activation(out=vt[:, m % 2, co:co + cw],
                                 in_=pV[:, co:co + cw], func=AF.Identity,
                                 bias=CONSTS[:, 0:1], scale=1.0)
                s.nop().then_inc(s_vev, 1)
            binc(s, 's')

        @blk.tensor
        def _(t):
            bwait(t)
            for m in range(NT):
                dwait(t, s_xnt, m)
                t.nop()
                if m >= 1:
                    t.wait_ge(s_gev, m)
                    t.nop()
                for co, cw in DCH:
                    for kd in range(ND):
                        t.matmul(out=pG[:, co:co + cw],
                                 lhsT=xnTt[:, m % 2, kd, :],
                                 rhs=WgT[:, kd, co:co + cw],
                                 start=(kd == 0), stop=False)
                    inst = t.matmul(out=pG[:, co:co + cw], lhsT=ONESB[:],
                                    rhs=BROWB[:, 0, co:co + cw],
                                    start=False, stop=True)
                t.drain().then_inc(s_gmm, 1)
                if m >= 1:
                    t.wait_ge(s_vev, m)
                    t.nop()
                for co, cw in DCH:
                    for kd in range(ND):
                        t.matmul(out=pV[:, co:co + cw],
                                 lhsT=xnTt[:, m % 2, kd, :],
                                 rhs=WvT[:, kd, co:co + cw],
                                 start=(kd == 0), stop=False)
                    inst = t.matmul(out=pV[:, co:co + cw], lhsT=ONESB[:],
                                    rhs=BROWB[:, 1, co:co + cw],
                                    start=False, stop=True)
                t.drain().then_inc(s_vmm, 1)
                t.wait_ge(s_ub, m + 1)
                t.nop()
                if m >= 1:
                    t.wait_ge(s_dcr, m)
                    t.nop()
                for co, cw in DCH:
                    t.matmul(out=pW[:, co:co + cw], lhsT=TRIs[:, 0, :],
                             rhs=ub[:, m % 2, co:co + cw], start=True,
                             stop=(m == 0))
                    if m >= 1:
                        t.matmul(out=pW[:, co:co + cw], lhsT=TRIs[:, 1, :],
                                 rhs=ub[:, (m - 1) % 2, co:co + cw],
                                 start=False, stop=True)
                t.drain().then_inc(s_wmm, 1)
            binc(t, 't')

        @blk.gpsimd
        def _(g):
            g = DW(g)
            bwait(g)
            for m in range(NT):
                if debug:
                    g.wait_ge(s_xnf, m + 1)
                    g.dma_start(out=xn_d[m * 128:(m + 1) * 128, :],
                                in_=xnf[:, m % 2, :]).then_inc(s_xnd[m % 2], 16)
                    g.dma_start(out=lns_d[m],
                                in_=lns[:]).then_inc(s_xnd[m % 2], 16)
                g.wait_ge(s_vev, m + 1)
                dinc(g.dma_start(out=vsp[m], in_=vt[:, m % 2, :]), s_vsp, m)
                g.wait_ge(s_xb, m + 1)
                dinc(g.dma_start(out=xb_d[m * 128:(m + 1) * 128, :],
                                 in_=xbt[:, m % 2, :]), s_xbs, m)
                dwait(g, s_xqt, m)
                dinc(g.dma_start(out=xqt_d[:, :, m * 128:(m + 1) * 128].rearrange("c p q -> p c q"),
                                 in_=txq[:, m % 2, :, :]), s_xqs, m)
            dwait_all(g, s_vsp, NT)
            dwait_all(g, s_xbs, NT)
            dwait_all(g, s_xqs, NT)
            binc(g, 'g')

        NBAR[0] += 1
        for _ds in (s_x, s_xnt, s_xqt, s_ut, s_xqs, s_xbs, s_vsp):
            PH_DMA.append((_ds, NT))
        sc1.close()
        scW.close()
        if maxph == 1:
            scU.close()
            return nc

        scS = contextlib.ExitStack()
        pS_ = scS.enter_context
        pSI = pS_(nc.psum_tensor("pSI", [128, 512], F32))
        pSB = pS_(nc.psum_tensor("pSB", [128, 512], F32))
        rcd = pS_(nc.sbuf_tensor("rcd", [128, NT, H], F32))
        colr = pS_(nc.sbuf_tensor("colr", [128, H], F32))
        stsc = pS_(nc.sbuf_tensor("stsc", [128, 2, NT, H], F32))

        @blk.vector
        def _(v):
            v = DW(v)
            bwait(v)
            v.tensor_scalar(out=den[:], in0=Rbuf[:], scalar1=1.0 / 8192.0,
                            scalar2=1.0, op0=ALU.mult, op1=ALU.add)
            emit_recip(v, rcd[:].rearrange("p a b -> p (a b)"),
                       den[:].rearrange("p a b -> p (a b)"),
                       stsc[:, 0].rearrange("p a b -> p (a b)"), NT * H)
            v.tensor_reduce(out=colr[:, 0:H].unsqueeze(2),
                            in_=rcd[:].rearrange("p m h -> p h m"),
                            axis=AX.X, op=ALU.add).then_inc(s_st, 1)
            v.wait_ge(s_st, 2)
            v.tensor_copy(out=sinv[:], in_=pSI[0:1, 0:H]).then_inc(s_st, 1)
            v.wait_ge(s_st, 4)
            v.tensor_copy(out=sinvb[:], in_=pSB[:, 0:H])
            sB = sinvb[:].unsqueeze(1).broadcast_to([128, NT, H])
            v.tensor_tensor(out=stsc[:, 1], in0=den[:], in1=sB, op=ALU.mult)
            emit_recip(v, gf[:].rearrange("p a b -> p (a b)"),
                       stsc[:, 1].rearrange("p a b -> p (a b)"),
                       stsc[:, 0].rearrange("p a b -> p (a b)"), NT * H)
            v.tensor_scalar(out=gfh[:], in0=gf[:], scalar1=0.5, scalar2=None,
                            op0=ALU.mult).then_inc(s_st, 1)
            binc(v, 'v')

        @blk.tensor
        def _(t):
            bwait(t)
            t.wait_ge(s_st, 1)
            t.nop()
            t.matmul(out=pSI[0:1, 0:H], lhsT=ONESC[:], rhs=colr[:, 0:H],
                     start=True, stop=True)
            t.drain().then_inc(s_st, 1)
            t.wait_ge(s_st, 3)
            t.nop()
            t.matmul(out=pSB[:, 0:H], lhsT=ONESR[:], rhs=sinv[:],
                     start=True, stop=True)
            t.drain().then_inc(s_st, 1)
            binc(t, 't')

        @blk.scalar
        def _(s):
            s = DW(s)
            binc(s, 's')

        @blk.sync
        def _(sy):
            binc(sy, 'sy')

        @blk.gpsimd
        def _(g):
            g = DW(g)
            binc(g, 'g')

        NBAR[0] += 1
        scS.close()
        if maxph == 2:
            scU.close()
            return nc

        scA = contextlib.ExitStack()
        pA_ = scA.enter_context
        at_s = pA_(nc.sbuf_tensor("at_s", [128, 2, D], BF16))
        Embuf = pA_(nc.sbuf_tensor("Embuf", [128, 2, H, 256], BF16))
        EmT = pA_(nc.sbuf_tensor("EmT", [128, 2, 2 * H, 128], BF16))
        vwin = pA_(nc.sbuf_tensor("vwin", [128, 2, D], BF16))
        smt = pA_(nc.sbuf_tensor("smt", [128, 2, 256], F32))
        DENS = pA_(nc.sbuf_tensor("DENS", [128, H], F32))
        DSC = pA_(nc.sbuf_tensor("DSC", [128, H], F32))
        rdt = pA_(nc.sbuf_tensor("rdt", [128, 2, H], F32))
        pSc = [pA_(nc.psum_tensor(f"pSc{i}", [128, 512], F32)) for i in range(2)]
        pAV = [pA_(nc.psum_tensor(f"pAV{i}", [128, 512], F32)) for i in range(2)]

        def hrow(h):
            return slice(64 * (h % 2), 64 * (h % 2) + 64)

        @blk.tensor
        def _(t):
            bwait(t)
            for m in range(NT):
                qw = slice(PAD + m * 128, PAD + (m + 1) * 128)
                kw = slice(PAD + (m - 1) * 128, PAD + (m + 1) * 128)
                for h in range(H):
                    i = m * H + h
                    if i >= 2:
                        t.wait_ge(s_nm, i - 1)
                        t.nop()
                    hc = h // 2
                    t.matmul(out=pSc[h % 2][:, 0:256], lhsT=uT[hrow(h), hc, qw],
                             rhs=uT[hrow(h), hc, kw], start=True, stop=True)
                    t.drain().then_inc(s_sc, 1)
                dwait(t, s_emt, m)
                t.nop()
                dwait(t, s_vw, m)
                t.nop()
                for h in range(H):
                    i = m * H + h
                    if i >= 2:
                        t.wait_ge(s_ave, i - 1)
                        t.nop()
                    vh = slice(h * HD, (h + 1) * HD)
                    if m >= 1:
                        t.matmul(out=pAV[h % 2][:, 0:HD],
                                 lhsT=EmT[:, m % 2, 2 * h, :],
                                 rhs=vwin[:, (m - 1) % 2, vh],
                                 start=True, stop=False)
                    t.matmul(out=pAV[h % 2][:, 0:HD],
                             lhsT=EmT[:, m % 2, 2 * h + 1, :],
                             rhs=vwin[:, m % 2, vh],
                             start=(m == 0), stop=True)
                    t.drain().then_inc(s_av, 1)
            binc(t, 't')

        @blk.vector
        def _(v):
            v = DW(v)
            bwait(v)
            for m in range(NT):
                sel = 1 if m == 0 else 0
                for h in range(H):
                    i = m * H + h
                    v.wait_ge(s_sc, i + 1)
                    if i >= 2:
                        v.wait_ge(s_exp, i - 1)
                    v.scalar_tensor_tensor(out=smt[:, i % 2, :],
                                           in0=pSc[h % 2][:, 0:256], scalar=0.5,
                                           in1=NEGMs[:, sel, :],
                                           op0=ALU.mult,
                                           op1=ALU.add).then_inc(s_nm, 1)
                v.wait_ge(s_exp, H * (m + 1))
                emit_recip(v, rdt[:, m % 2, :], DENS[:], DSC[:], H)
                v.drain().then_inc(s_rd, 1)
                for h in range(H):
                    i = m * H + h
                    v.wait_ge(s_av, i + 1)
                    if m >= 2 and h == 0:
                        dwait(v, s_ats, m - 2)
                    v.tensor_scalar(out=at_s[:, m % 2, h * HD:(h + 1) * HD],
                                    in0=pAV[h % 2][:, 0:HD],
                                    scalar1=rdt[:, m % 2, h:h + 1],
                                    scalar2=None,
                                    op0=ALU.mult).then_inc(s_ave, 1)
            binc(v, 'v')

        @blk.scalar
        def _(s):
            s = DW(s)
            bwait(s)
            for m in range(NT):
                if m >= 1:
                    s.wait_ge(s_rd, m)
                if m >= 2:
                    dwait(s, s_emt, m - 2)
                for h in range(H):
                    i = m * H + h
                    s.wait_ge(s_nm, i + 1)
                    s.activation(out=Embuf[:, m % 2, h, :],
                                 in_=smt[:, i % 2, :], func=AF.Exp,
                                 bias=gfh[:, m, h:h + 1],
                                 scale=gf[:, m, h:h + 1],
                                 accum_out=DENS[:, h:h + 1]).then_inc(s_exp, 1)
            binc(s, 's')

        @blk.gpsimd
        def _(g):
            g = DW(g)
            bwait(g)
            for m in range(NT):
                g.wait_ge(s_ave, H * (m + 1))
                dinc(g.dma_start(out=at_d[m], in_=at_s[:, m % 2, :]), s_ats, m)
            dwait_all(g, s_ats, NT)
            binc(g, 'g')

        @blk.sync
        def _(sy):
            bwait(sy)
            for m in range(NT):
                if m >= 2:
                    sy.wait_ge(s_av, H * m)
                dinc(sy.dma_start(out=vwin[:, m % 2, :], in_=vsp[m]), s_vw, m)
                sy.wait_ge(s_exp, H * (m + 1))
                if m >= 2:
                    sy.wait_ge(s_av, H * (m - 1))
                dinc(sy.dma_start_transpose(EmT[:, m % 2, :, :],
                                            Embuf[:, m % 2, :, :]), s_emt, m)
            binc(sy, 'sy')

        NBAR[0] += 1
        for _ds in (s_vw, s_emt, s_ats):
            PH_DMA.append((_ds, NT))
        scA.close()
        scU.close()
        if maxph == 3:
            return nc

        scP3 = contextlib.ExitStack()
        p3_ = scP3.enter_context
        at_T = p3_(nc.sbuf_tensor("at_T", [128, ND, T], BF16))
        atl = p3_(nc.sbuf_tensor("atl", [128, 2, D], BF16))
        xbl = p3_(nc.sbuf_tensor("xbl", [128, 2, D], F32))
        ptt = p3_(nc.sbuf_tensor("ptt", [128, 2, D], F32))
        wotm3 = p3_(nc.sbuf_tensor("wotm3", [128, 2, D], F32))
        wobf3 = p3_(nc.sbuf_tensor("wobf3", [128, 2, D], BF16))
        WoTp = p3_(nc.sbuf_tensor("WoTp", [128, ND, D], BF16))
        pP = [p3_(nc.psum_tensor(f"pP{i}", [128, D], F32)) for i in range(2)]

        @blk.scalar
        def _(s):
            s = DW(s)
            bwait(s)
            for wi in range(ND):
                if wi >= 2:
                    s.wait_ge(s_woc, wi - 1)
                dinc(s.dma_start(out=wotm3[:, wi % 2, :],
                                 in_=wo[wi * 128:(wi + 1) * 128, :]), s_wol, wi)
            for m in range(NT):
                if m >= 2:
                    s.wait_ge(s_p3e, m - 1)
                dinc(s.dma_start(out=xbl[:, m % 2, :],
                                 in_=xb_d[m * 128:(m + 1) * 128, :]), s_xbl, m)
            binc(s, 's')

        @blk.gpsimd
        def _(g):
            g = DW(g)
            bwait(g)
            for wi in range(ND):
                dwait(g, s_wol, wi)
                if wi >= 2:
                    dwait(g, s_wot, wi - 2)
                g.tensor_copy(out=wobf3[:, wi % 2, :],
                              in_=wotm3[:, wi % 2, :]).then_inc(s_woc, 1)
            for m in range(NT):
                g.wait_ge(s_p3e, m + 1)
                dinc(g.dma_start(out=part_d[m * 128:(m + 1) * 128, :],
                                 in_=ptt[:, m % 2, :]), s_ps, m)
            dwait_all(g, s_ps, NT)
            binc(g, 'g')

        @blk.sync
        def _(sy):
            bwait(sy)
            for wi in range(ND):
                sy.wait_ge(s_woc, wi + 1)
                dinc(sy.dma_start_transpose(WoTp[:, :, wi * 128:(wi + 1) * 128],
                                            wobf3[:, wi % 2, :]), s_wot, wi)
            for m in range(NT):
                if m >= 2:
                    sy.wait_ge(s_p3m, m - 1)
                sy.dma_start(out=atl[:, m % 2, :],
                             in_=at_d[m]).then_inc(s_att[m % 2], 16)
                sy.wait_ge(s_att[m % 2], 32 * (m // 2) + 16)
                sy.dma_start_transpose(at_T[:, :, m * 128:(m + 1) * 128],
                                       atl[:, m % 2, :]).then_inc(s_att[m % 2], 16)
            binc(sy, 'sy')

        @blk.tensor
        def _(t):
            bwait(t)
            dwait_all(t, s_wot, ND)
            t.nop()
            for m in range(NT):
                t.wait_ge(s_att[m % 2], 32 * (m // 2 + 1))
                t.nop()
                if m >= 2:
                    t.wait_ge(s_p3e, m - 1)
                    t.nop()
                for co, cw in DCH:
                    for kd in range(ND):
                        inst = t.matmul(out=pP[m % 2][:, co:co + cw],
                                        lhsT=at_T[:, kd, m * 128:(m + 1) * 128],
                                        rhs=WoTp[:, kd, co:co + cw],
                                        start=(kd == 0), stop=(kd == ND - 1))
                t.drain().then_inc(s_p3m, 1)
            binc(t, 't')

        @blk.vector
        def _(v):
            v = DW(v)
            bwait(v)
            for m in range(NT):
                v.wait_ge(s_p3m, m + 1)
                dwait(v, s_xbl, m)
                if m >= 2:
                    dwait(v, s_ps, m - 2)
                v.tensor_tensor(out=ptt[:, m % 2, :], in0=pP[m % 2][:],
                                in1=xbl[:, m % 2, :],
                                op=ALU.add).then_inc(s_p3e, 1)
            binc(v, 'v')

        NBAR[0] += 1
        for _ds, _n in ((s_wol, ND), (s_wot, ND), (s_att, 2 * NT),
                        (s_xbl, NT), (s_ps, NT)):
            PH_DMA.append((_ds, _n))
        scP3.close()
        if maxph == 4:
            return nc

        scF0 = contextlib.ExitStack()
        f0_ = scF0.enter_context
        fwt = f0_(nc.sbuf_tensor("fwt", [128, 2, II], F32))
        fq1 = f0_(nc.sbuf_tensor("fq1", [128, II], F32))
        fq2 = f0_(nc.sbuf_tensor("fq2", [128, II], F32))
        fq3 = f0_(nc.sbuf_tensor("fq3", [128, II], F32))
        fgm = f0_(nc.sbuf_tensor("fgm", [128, 4, GI], F32))
        fwq = f0_(nc.sbuf_tensor("fwq", [128, 2, II], BF16))
        ftw = f0_(nc.sbuf_tensor("ftw", [128, 2, NIC, 128], BF16))

        NWG = II // 128
        wtiles = ([("g", r) for r in range(NWG)] + [("u", r) for r in range(NWG)]
                  + [("d", r) for r in range(ND)])
        NWT = len(wtiles)

        @blk.scalar
        def _(s):
            s = DW(s)
            bwait(s)
            nc.scalar.preload_act_table(AF.Silu)
            for i, (kind, r) in enumerate(wtiles):
                if i >= 2:
                    s.wait_ge(s_fwq, i - 1)
                src = dict(g=wgate, u=wup, d=wdown)[kind]
                wd = D if kind in ("g", "u") else II
                dinc(s.dma_start(out=fwt[:, i % 2, 0:wd],
                                 in_=src[r * 128:(r + 1) * 128, :]), s_fwl, i)
            binc(s, 's')

        @blk.vector
        def _(v):
            v = DW(v)
            bwait(v)
            for i, (kind, r) in enumerate(wtiles):
                wd = D if kind in ("g", "u") else II
                dwait(v, s_fwl, i)
                if i >= 2:
                    dwait(v, s_fwt, i - 2)
                emit_qdq(v, fwt[:, i % 2, 0:wd], fwt[:, i % 2, 0:wd],
                         fwq[:, i % 2, 0:wd], fq1, fq2, fq3, fgm,
                         wd).then_inc(s_fwq, 1)
            binc(v, 'v')

        @blk.sync
        def _(sy):
            bwait(sy)
            for i, (kind, r) in enumerate(wtiles):
                nch = ND if kind in ("g", "u") else NIC
                sy.wait_ge(s_fwq, i + 1)
                if i >= 2:
                    dwait(sy, s_fws, i - 2)
                dinc(sy.dma_start_transpose(ftw[:, i % 2, 0:nch, :],
                                            fwq[:, i % 2, 0:nch * 128]), s_fwt, i)
            binc(sy, 'sy')

        @blk.gpsimd
        def _(g):
            g = DW(g)
            bwait(g)
            for i, (kind, r) in enumerate(wtiles):
                nch = ND if kind in ("g", "u") else NIC
                dst = dict(g=wgq_d, u=wuq_d, d=wdq_d)[kind]
                dwait(g, s_fwt, i)
                dinc(g.dma_start(out=dst[:, :, r * 128:(r + 1) * 128].rearrange("c p q -> p c q"),
                                 in_=ftw[:, i % 2, 0:nch, :]), s_fws, i)
            dwait_all(g, s_fws, NWT)
            binc(g, 'g')

        @blk.tensor
        def _(t):
            binc(t, 't')

        NBAR[0] += 1
        for _ds in (s_fwl, s_fwt, s_fws):
            PH_DMA.append((_ds, NWT))
        scF0.close()
        if maxph == 5:
            return nc

        scF1 = contextlib.ExitStack()
        f1_ = scF1.enter_context
        xqt_s = f1_(nc.sbuf_tensor("xqt_s", [128, ND, T // 2], BF16))
        wgq_s = f1_(nc.sbuf_tensor("wgq_s", [128, ND, II], BF16))
        wuq_s = f1_(nc.sbuf_tensor("wuq_s", [128, ND, II], BF16))
        hbuf = f1_(nc.sbuf_tensor("hbuf", [128, II], F32))
        h1 = f1_(nc.sbuf_tensor("h1", [128, II], F32))
        h2 = f1_(nc.sbuf_tensor("h2", [128, II], F32))
        h3 = f1_(nc.sbuf_tensor("h3", [128, II], F32))
        hgm = f1_(nc.sbuf_tensor("hgm", [128, 4, GI], F32))
        hqb = f1_(nc.sbuf_tensor("hqb", [128, 2, II], BF16))
        thq = f1_(nc.sbuf_tensor("thq", [128, 2, NIC, 128], BF16))
        pGa = [f1_(nc.psum_tensor(f"pGa{i}", [128, 512], F32)) for i in range(2)]
        pUp = [f1_(nc.psum_tensor(f"pUp{i}", [128, 512], F32)) for i in range(2)]

        @blk.sync
        def _(sy):
            bwait(sy)
            for kd in range(ND):
                sy.dma_start(out=wgq_s[:, kd, :], in_=wgq_d[kd]).then_inc(s_f1w, 16)
                sy.dma_start(out=wuq_s[:, kd, :], in_=wuq_d[kd]).then_inc(s_f1w, 16)
            for th in range(2):
                if th == 1:
                    sy.wait_ge(s_hq, NTH)
                for kd in range(ND):
                    dinc(sy.dma_start(out=xqt_s[:, kd, :],
                                      in_=xqt_d[kd, :, th * (T // 2):(th + 1) * (T // 2)]),
                         s_f1x, th * ND + kd)
                for mt in range(NTH):
                    mg = th * NTH + mt
                    sy.wait_ge(s_hq, mg + 1)
                    if f1mode == 1:
                        continue
                    if mg >= 2:
                        dwait(sy, s_hqs, mg - 2)
                    dinc(sy.dma_start_transpose(thq[:, mg % 2, :, :],
                                                hqb[:, mg % 2, :]), s_hqt, mg)
                    dwait(sy, s_hqt, mg)
                    dinc(sy.dma_start(out=hqt_d[mg],
                                      in_=thq[:, mg % 2, :, :].rearrange("p c q -> p (c q)")),
                         s_hqs, mg)
            binc(sy, 'sy')

        @blk.tensor
        def _(t):
            bwait(t)
            t.wait_ge(s_f1w, 16 * 2 * ND)
            t.nop()
            for th in range(2):
                dwait_all(t, s_f1x, ND * (th + 1))
                t.nop()
                for mt in range(NTH):
                    mg = th * NTH + mt
                    tsl = slice(mt * 128, (mt + 1) * 128)
                    for ci, (co, cw) in enumerate(ICH):
                        ii = mg * NIH + ci
                        if ii >= 2:
                            t.wait_ge(s_silu, ii - 1)
                            t.nop()
                        if f1mode != 4:
                            for kd in range(ND):
                                inst = t.matmul(out=pGa[ci % 2][:, 0:cw],
                                                lhsT=xqt_s[:, kd, tsl],
                                                rhs=wgq_s[:, kd, co:co + cw],
                                                start=(kd == 0),
                                                stop=(kd == ND - 1))
                        t.drain().then_inc(s_gmf, 1)
                        if ii >= 2:
                            t.wait_ge(s_hm, ii - 1)
                            t.nop()
                        if f1mode != 4:
                            for kd in range(ND):
                                inst = t.matmul(out=pUp[ci % 2][:, 0:cw],
                                                lhsT=xqt_s[:, kd, tsl],
                                                rhs=wuq_s[:, kd, co:co + cw],
                                                start=(kd == 0),
                                                stop=(kd == ND - 1))
                        t.drain().then_inc(s_umf, 1)
            binc(t, 't')

        @blk.scalar
        def _(s):
            s = DW(s)
            bwait(s)
            for mg in range(NT):
                for ci, (co, cw) in enumerate(ICH):
                    ii = mg * NIH + ci
                    s.wait_ge(s_gmf, ii + 1)
                    if mg >= 1 and ci == 0:
                        s.wait_ge(s_hq, mg)
                    if f1mode == 5:
                        s.drain().then_inc(s_silu, 1)
                    else:
                        s.activation(out=hbuf[:, co:co + cw],
                                     in_=pGa[ci % 2][:, 0:cw],
                                     func=(AF.Identity if f1mode == 7 else AF.Silu),
                                     bias=CONSTS[:, 0:1], scale=1.0).then_inc(s_silu, 1)
            binc(s, 's')

        @blk.vector
        def _(v):
            v = DW(v)
            bwait(v)
            if f1mode in (4, 5, 6):
                v.memset(hbuf[:], 0.5)
            for mg in range(NT):
                for ci, (co, cw) in enumerate(ICH):
                    ii = mg * NIH + ci
                    v.wait_ge(s_silu, ii + 1)
                    v.wait_ge(s_umf, ii + 1)
                    if f1mode in (5, 6):
                        v.drain().then_inc(s_hm, 1)
                    else:
                        v.tensor_tensor(out=hbuf[:, co:co + cw],
                                        in0=hbuf[:, co:co + cw],
                                        in1=pUp[ci % 2][:, 0:cw],
                                        op=ALU.mult).then_inc(s_hm, 1)
                if mg >= 2 and f1mode != 1:
                    dwait(v, s_hqt, mg - 2)
                if f1mode == 2:
                    v.memset(hqb[:, mg % 2, 0:II], 0.25).then_inc(s_hq, 1)
                elif f1mode >= 10:
                    emit_qdq(v, hbuf[:], hbuf[:], hqb[:, mg % 2, 0:II], h1, h2,
                             h3, hgm, II, nops=f1mode - 10)
                    v.memset(hqb[:, mg % 2, 0:II], 0.25).then_inc(s_hq, 1)
                else:
                    emit_qdq(v, hbuf[:], hbuf[:], hqb[:, mg % 2, 0:II], h1, h2,
                             h3, hgm, II).then_inc(s_hq, 1)
            binc(v, 'v')

        @blk.gpsimd
        def _(g):
            g = DW(g)
            bwait(g)
            binc(g, 'g')

        NBAR[0] += 1
        PH_DMA.extend([(s_f1w, 2 * ND), (s_f1x, 2 * ND), (s_hqt, NT),
                       (s_hqs, NT)])
        scF1.close()
        if maxph == 6:
            return nc

        scF2 = contextlib.ExitStack()
        f2_ = scF2.enter_context
        wdq_s = f2_(nc.sbuf_tensor("wdq_s", [128, NIC, D], BF16))
        hql = f2_(nc.sbuf_tensor("hql", [128, 2, NIC, 128], BF16))
        ptl = f2_(nc.sbuf_tensor("ptl", [128, 2, D], F32))
        ott = f2_(nc.sbuf_tensor("ott", [128, 2, D], F32))
        pD = [f2_(nc.psum_tensor(f"pD{i}", [128, D], F32)) for i in range(2)]

        @blk.sync
        def _(sy):
            bwait(sy)
            for ki in range(NIC):
                sy.dma_start(out=wdq_s[:, ki, :], in_=wdq_d[ki]).then_inc(s_f2w, 16)
            for m in range(NT):
                if m >= 2:
                    sy.wait_ge(s_dmm, m - 1)
                dinc(sy.dma_start(out=hql[:, m % 2, :, :].rearrange("p c q -> p (c q)"),
                                  in_=hqt_d[m]), s_f2h, m)
                if m >= 2:
                    sy.wait_ge(s_oe, m - 1)
                    dinc(sy.dma_start(out=o[(m - 2) * 128:(m - 1) * 128, :],
                                      in_=ott[:, m % 2, :]), s_os, m - 2)
            for m in (NT - 2, NT - 1):
                sy.wait_ge(s_oe, m + 1)
                dinc(sy.dma_start(out=o[m * 128:(m + 1) * 128, :],
                                  in_=ott[:, m % 2, :]), s_os, m)
            dwait_all(sy, s_os, NT)
            binc(sy, 'sy')

        @blk.scalar
        def _(s):
            s = DW(s)
            bwait(s)
            for m in range(NT):
                if m >= 2:
                    s.wait_ge(s_oe, m - 1)
                dinc(s.dma_start(out=ptl[:, m % 2, :],
                                 in_=part_d[m * 128:(m + 1) * 128, :]), s_f2p, m)
            binc(s, 's')

        @blk.tensor
        def _(t):
            bwait(t)
            t.wait_ge(s_f2w, 16 * NIC)
            t.nop()
            for m in range(NT):
                dwait(t, s_f2h, m)
                t.nop()
                if m >= 2:
                    t.wait_ge(s_oe, m - 1)
                    t.nop()
                for co, cw in DCH:
                    for ki in range(NIC):
                        inst = t.matmul(out=pD[m % 2][:, co:co + cw],
                                        lhsT=hql[:, m % 2, ki, :],
                                        rhs=wdq_s[:, ki, co:co + cw],
                                        start=(ki == 0), stop=(ki == NIC - 1))
                t.drain().then_inc(s_dmm, 1)
            binc(t, 't')

        @blk.vector
        def _(v):
            v = DW(v)
            bwait(v)
            for m in range(NT):
                v.wait_ge(s_dmm, m + 1)
                dwait(v, s_f2p, m)
                if m >= 2:
                    dwait(v, s_os, m - 2)
                v.tensor_tensor(out=ott[:, m % 2, :], in0=pD[m % 2][:],
                                in1=ptl[:, m % 2, :],
                                op=ALU.add).then_inc(s_oe, 1)
            binc(v, 'v')

        @blk.gpsimd
        def _(g):
            g = DW(g)
            bwait(g)
            dwait_all(g, s_os, NT)

        scF2.close()

    return nc


def make_consts(C):
    jj = np.arange(128)[:, None]
    qq = np.arange(128)[None, :]
    tril = (jj <= qq).astype(ml_dtypes.bfloat16)
    triu = (jj >= qq).astype(ml_dtypes.bfloat16)
    tri = np.stack([tril, triu])
    qi = np.arange(128)[:, None]
    jl = np.arange(256)[None, :]
    main = np.where(((jl < 128) & (jl >= qi)) | ((jl >= 128) & (jl - 128 <= qi)),
                    0.0, NEG).astype(np.float32)
    t0 = np.where((jl >= 128) & (jl - 128 <= qi), 0.0, NEG).astype(np.float32)
    negm = np.stack([main, t0])
    onesb = np.ones((1, 128), ml_dtypes.bfloat16)
    onesc = np.ones((128, 1), np.float32)
    onesr = np.ones((1, 128), np.float32)
    icon = np.zeros((128, 6), np.int32)
    icon[:, 0] = MADD
    icon[:, 1] = MMASK
    icon[:, 2] = SGNMASK
    icon[:, 3] = 0x7FFFFFFF
    icon[:, 4] = 0x7EF311C3
    fcon = np.zeros((128, 2), np.float32)
    fcon[:, 1] = 1e-5
    return tri, negm, onesb, onesc, onesr, icon, fcon


def kernel(L0_attn_x, L0_attn_Wg, L0_attn_bg, L0_attn_Wv, L0_attn_bv,
           L0_attn_Wo, L0_attn_bo, L0_ffn_w_gate, L0_ffn_w_up, L0_ffn_w_down):
    C = FULL
    B = L0_attn_x.shape[0]
    tri, negm, onesb, onesc, onesr, icon, fcon = make_consts(C)
    nc = build_nc(C, debug=_DEBUG)
    base = {
        "wg": np.ascontiguousarray(L0_attn_Wg, np.float32),
        "wv": np.ascontiguousarray(L0_attn_Wv, np.float32),
        "wo": np.ascontiguousarray(L0_attn_Wo, np.float32),
        "bgr": np.ascontiguousarray(L0_attn_bg, np.float32).reshape(1, -1),
        "bvr": np.ascontiguousarray(L0_attn_bv, np.float32).reshape(1, -1),
        "bor": np.ascontiguousarray(L0_attn_bo, np.float32).reshape(1, -1),
        "wgate": np.ascontiguousarray(L0_ffn_w_gate, np.float32),
        "wup": np.ascontiguousarray(L0_ffn_w_up, np.float32),
        "wdown": np.ascontiguousarray(L0_ffn_w_down, np.float32),
        "tri": tri, "negm": negm, "onesb": onesb, "onesc": onesc,
        "onesr": onesr, "icon": icon, "fcon": fcon,
    }
    in_maps = []
    for b in range(B):
        m = dict(base)
        m["x"] = np.ascontiguousarray(L0_attn_x[b], np.float32)
        in_maps.append(m)
    res = run_bass_kernel_spmd(nc, in_maps, core_ids=list(range(B)))
    return np.stack([np.asarray(r["o"], np.float32) for r in res.results])
